# version 01
# baseline (speedup 1.0000x reference)
import contextlib
import numpy as np
import ml_dtypes
import concourse.bass as bass
import concourse.mybir as mybir

F32 = mybir.dt.float32
BF16 = mybir.dt.bfloat16
AF = mybir.ActivationFunctionType
ALU = mybir.AluOpType
BF = ml_dtypes.bfloat16

TOK = 2048
FULL = 4096
C = 1024
NCH = 8
H = 16
HD = 64
NPAIR = 8
HID = 2048
NHC = 16
E = 8
R = 16
TB = 512
NTB = TOK // TB
NTILE = TOK // 128
KT = 128
NKT_OWN = TOK // KT
NKT = FULL // KT
BIGM = 1.0e30
EPS = 1.0e-5

ENGINES = ("tensor", "vector", "scalar", "gpsimd", "sync")


class Emitter:
    def __init__(self, nc):
        self.nc = nc
        self.q = {e: [] for e in ENGINES}
        self.count = {}
        self.sem_names = []

    def sem(self, name):
        if name not in self.count:
            self.count[name] = 0
            self.sem_names.append(name)
        return name

    def emit(self, eng, fn, sem=None, inc=1):
        if sem is not None:
            self.sem(sem)
            self.count[sem] += inc

            def wrapped(engine, sems, fn=fn, sem=sem, inc=inc):
                fn(engine, sems).then_inc(sems[sem], inc)
            self.q[eng].append(wrapped)
        else:
            self.q[eng].append(lambda engine, sems, fn=fn: fn(engine, sems))

    def wait(self, eng, sem, val=None):
        self.sem(sem)
        if val is None:
            val = self.count[sem]
        assert val <= self.count[sem], f"wait {sem}>={val} but only {self.count[sem]} emitted"
        self.q[eng].append(lambda engine, sems, sem=sem, val=val: engine.wait_ge(sems[sem], val))

    def dma(self, eng, out, in_, sem, **kw):
        self.emit(eng, lambda e, s, out=out, in_=in_, kw=kw: e.dma_start(out=out, in_=in_, **kw),
                  sem=sem, inc=16)

    def c(self, sem):
        return self.count[self.sem(sem)]

    def finalize(self):
        nc = self.nc
        with contextlib.ExitStack() as stack:
            sems = {n: stack.enter_context(nc.semaphore(n)) for n in self.sem_names}
            block = stack.enter_context(nc.Block())
            for eng_name in ENGINES:
                ops = self.q[eng_name]
                if not ops:
                    continue

                def make_body(ops):
                    def body(engine):
                        for op in ops:
                            op(engine, sems)
                    return body
                getattr(block, eng_name)(make_body(ops))


class Arena:

    def __init__(self, handle, ncols):
        self.h = handle
        self.ncols = ncols
        self.pos = 0
        self.marks = {}

    def alloc(self, ncols, name=""):
        self.pos = (self.pos + 15) // 16 * 16
        assert self.pos + ncols <= self.ncols, f"arena overflow at {name}: {self.pos}+{ncols}>{self.ncols}"
        view = self.h[:, self.pos:self.pos + ncols]
        self.pos += ncols
        return view

    def mark(self, name):
        self.marks[name] = self.pos

    def reset(self, name=None, pos=0):
        self.pos = self.marks[name] if name is not None else pos


def mm(E, out, lhsT, rhs, start, stop, sem=None, inc=1):
    E.emit("tensor", lambda e, s, o=out, l=lhsT, r=rhs, st=start, sp=stop:
           e.matmul(o, l, r, start=st, stop=sp), sem=sem, inc=inc)


def transpose(E, out, in_, ident, sem=None, inc=1):
    E.emit("tensor", lambda e, s, o=out, i=in_, d=ident: e.transpose(o, i, d), sem=sem, inc=inc)


def vcopy(E, out, in_, sem=None, inc=1):
    E.emit("vector", lambda e, s, o=out, i=in_: e.tensor_copy(out=o, in_=i), sem=sem, inc=inc)


def vtt(E, out, in0, in1, op, sem=None, inc=1):
    E.emit("vector", lambda e, s, o=out, a=in0, b=in1, p=op:
           e.tensor_tensor(out=o, in0=a, in1=b, op=p), sem=sem, inc=inc)


def vts(E, out, in0, s1, s2=None, op0=ALU.mult, op1=None, accum=None, sem=None, inc=1):
    kw = {}
    if op1 is not None:
        kw["op1"] = op1
    E.emit("vector", lambda e, s, o=out, a=in0, s1=s1, s2=s2, p0=op0, kw=kw, acc=accum:
           e.tensor_scalar(out=o, in0=a, scalar1=s1, scalar2=s2, op0=p0, accum_out=acc, **kw),
           sem=sem, inc=inc)


def vstt(E, out, in0, scal, in1, op0, op1, accum=None, sem=None, inc=1):
    E.emit("vector", lambda e, s, o=out, a=in0, sc=scal, b=in1, p0=op0, p1=op1, acc=accum:
           e.scalar_tensor_tensor(out=o, in0=a, scalar=sc, in1=b, op0=p0, op1=p1, accum_out=acc),
           sem=sem, inc=inc)


def vreduce(E, out, in_, op, sem=None, inc=1):
    if op == "max":
        E.emit("vector", lambda e, s, o=out, i=in_: e.reduce_max(o, i, axis=mybir.AxisListType.X),
               sem=sem, inc=inc)
    else:
        E.emit("vector", lambda e, s, o=out, i=in_: e.reduce_sum(o, i, axis=mybir.AxisListType.X),
               sem=sem, inc=inc)


def act(E, out, in_, func, bias=None, scale=None, accum=None, sem=None, inc=1):
    kw = {}
    if bias is not None:
        kw["bias"] = bias
    if scale is not None:
        kw["scale"] = scale
    if accum is not None:
        kw["accum_out"] = accum
    E.emit("scalar", lambda e, s, o=out, i=in_, f=func, kw=kw:
           e.activation(out=o, in_=i, func=f, **kw), sem=sem, inc=inc)


def scopy(E, out, in_, sem=None, inc=1):
    E.emit("scalar", lambda e, s, o=out, i=in_: e.copy(out=o, in_=i), sem=sem, inc=inc)


def memset(E, eng, ap, val, sem=None, inc=1):
    E.emit(eng, lambda e, s, a=ap, v=val: e.memset(a, v), sem=sem, inc=inc)


def ap3(view, off, dims):
    return bass.AP(view.tensor, view.offset + off, [list(view.ap[0])] + [list(d) for d in dims])


def bcast_free(col, n):
    return bass.AP(col.tensor, col.offset, [list(col.ap[0]), [0, n]])


class MKProgram:

    def __init__(self, mode, debug_dumps=()):
        self.mode = mode
        self.debug_dumps = list(debug_dumps)
        self.nc = bass.Bass("TRN2", target_bir_lowering=False, debug=False)
        self.E = Emitter(self.nc)
        self.dram = {}
        self.ext_inputs = set()
        self.declare_dram()

    def din(self, name, shape, dtype=F32):
        t = self.nc.dram_tensor(name, list(shape), dtype, kind="ExternalInput")
        self.dram[name] = t.ap()
        self.ext_inputs.add(name)
        return self.dram[name]

    def dout(self, name, shape, dtype=F32):
        t = self.nc.dram_tensor(name, list(shape), dtype, kind="ExternalOutput")
        self.dram[name] = t.ap()
        return self.dram[name]

    def dint(self, name, shape, dtype=F32):
        t = self.nc.dram_tensor(name, list(shape), dtype)
        self.dram[name] = t.ap()
        return self.dram[name]

    def declare_dram(self):
        d = self.din
        mode = self.mode
        d("x_own", [TOK, C])
        d("x_oth", [TOK, C])
        if mode in ("l0", "fused"):
            for nm in ("l0_Wq", "l0_Wk", "l0_Wv", "l0_Wo"):
                d(nm, [C, C])
            for nm in ("l0_bq", "l0_bk", "l0_bv", "l0_bo"):
                d(nm, [C])
            d("m_Wg", [E, C])
            d("m_W1", [E, HID, C]); d("m_W3", [E, HID, C]); d("m_W2", [E, C, HID])
            d("m_b1", [E, HID]); d("m_b3", [E, HID]); d("m_b2", [E, C])
            d("m_A1", [E, R, C]); d("m_A3", [E, R, C]); d("m_A2", [E, R, HID])
            d("m_B1", [E, HID, R]); d("m_B3", [E, HID, R]); d("m_B2", [E, C, R])
            self.dint("w1s", [E, HID, C], BF16)
            self.dint("w3s", [E, HID, C], BF16)
            self.dint("w2s", [E, C, HID], BF16)
        if mode in ("l1", "fused"):
            d("l1_Wq", [C, C]); d("l1_Wk", [C, HD]); d("l1_Wv", [C, HD]); d("l1_Wo", [C, C])
            d("l1_bq", [C]); d("l1_bk", [HD]); d("l1_bv", [HD]); d("l1_bo", [C])
            d("r_Wr", [C, C]); d("r_Wk", [C, C]); d("r_Wv", [C, C])
            d("r_mur", [C]); d("r_muk", [C])
            d("rope_cos", [FULL, HD]); d("rope_sin", [FULL, HD])
            self.dint("wrs", [C, C], BF16)
            self.dint("wks", [C, C], BF16)
            self.dint("wvs", [C, C], BF16)
        if mode == "fused":
            self.dint("cc_in", [TOK, C], F32)
            self.dint("cc_out", [FULL, C], F32)
        d("ident_bf", [128, 128], BF16)
        d("ident_f32", [128, 128], F32)
        d("flags", [128, 4])
        d("diag", [128, 4 * TB], BF16)
        self.dout("out", [TOK, C], F32)
        for (nm, shape, dt) in self.debug_dumps:
            self.dout(nm, shape, dt)

    def alloc(self, stack):
        nc = self.nc
        NB = 68608
        NF = 18400
        self.arb_h = stack.enter_context(nc.sbuf_tensor("arb", [128, NB], BF16))
        self.arf_h = stack.enter_context(nc.sbuf_tensor("arf", [128, NF], F32))
        self.arb = Arena(self.arb_h, NB)
        self.arf = Arena(self.arf_h, NF)
        self.xf = self.arf.alloc(NCH * TOK, "xf")
        self.tok = self.arf.alloc(C, "tok")
        self.sm_buf = self.arf.alloc(320, "sm")
        self.row = self.arf.alloc(512, "row")
        self.idf = self.arf.alloc(128, "idf")
        self.arf.mark("f_end")
        self.xn = self.arb.alloc(NCH * (TOK + 1), "xn")
        self.idb = self.arb.alloc(128, "idb")
        self.gatesT = self.arb.alloc(TOK, "gatesT")
        self.onesrow_bf = self.arb.alloc(128, "onesrow")
        self.arb.mark("persist")
        self.psf = [stack.enter_context(nc.psum_tensor(f"ps{i}", [128, 512], F32)) for i in range(6)]
        self.pst = [stack.enter_context(nc.psum_tensor(f"pt{i}", [128, 1024], BF16)) for i in range(2)]
        self.sm_pos = 0

    def xn_ap(self, cc, lo, n):
        return self.xn[:, cc * (TOK + 1) + 1 + lo: cc * (TOK + 1) + 1 + lo + n]

    def smslot(self, n=1, name=""):
        assert self.sm_pos + n <= 320, f"sm overflow {name}"
        v = self.sm_buf[:, self.sm_pos:self.sm_pos + n]
        self.sm_pos += n
        return v

    def load_consts(self):
        E_, d = self.E, self.dram
        E_.dma("gpsimd", self.idb[:, :], d["ident_bf"], "cl")
        E_.dma("gpsimd", self.idf[:, :], d["ident_f32"], "cl")
        self.flags = self.smslot(4, "flags")
        E_.dma("gpsimd", self.flags, d["flags"], "cl")
        self.epscol = self.smslot(1, "eps")
        memset(E_, "vector", self.epscol, EPS, sem="cst")
        memset(E_, "vector", self.onesrow_bf[0:1, :], 1.0, sem="cst")

    def barrier_consts(self):
        E_ = self.E
        for eng in ENGINES:
            E_.wait(eng, "cl", E_.c("cl"))
            E_.wait(eng, "cst", E_.c("cst"))

    def setup_stream_transients(self):
        arb = self.arb
        self.p0_sq = arb.alloc(1024, "p0_sq")
        self.p0_xntok = arb.alloc(2048, "p0_xntok")
        self.p0_tok2 = arb.alloc(1024, "p0_tok2")
        self.p0_gb = arb.alloc(16, "p0_gb")
        s = self.smslot
        self.st_ssum = s(1, "ssum"); self.st_ssq = s(1, "ssq"); self.st_m = s(1, "m")
        self.st_v = s(1, "v"); self.st_m2 = s(1, "m2"); self.st_std = s(1, "std")
        self.st_r = s(1, "r"); self.st_negm = s(1, "negm"); self.st_rm = s(1, "rm")

    def emit_stream(self, tag, src_rows, dst_xn_buf, xn_width, lead, ntiles,
                    build_xf=False, build_gates=False, src2_rows=None,
                    dst_tile_off=0):
        E_ = self.E
        sld = tag + "l"
        sve = tag + "v"
        ssc = tag + "s"
        spe = tag + "p"
        strv = tag + "t"
        t = self.tok[:, :]
        for (gs, gv) in getattr(self, "tok_guard", []):
            self.E.wait("gpsimd", gs, gv)
        xn_tok = self.p0_xntok
        tok_free_pe = 0
        tok_free_ve = 0
        bank_free = [0, 0]
        self.xf_ready = []
        sxf = tag + "x"
        for i in range(ntiles):
            if i > 0:
                E_.wait("gpsimd", sve, tok_free_ve)
                if build_xf:
                    E_.wait("gpsimd", spe, tok_free_pe)
            E_.dma("gpsimd", t, src_rows[i * 128:(i + 1) * 128, :], sld)
            if src2_rows is not None:
                E_.dma("gpsimd", self.p0_tok2[:, :], src2_rows[i * 128:(i + 1) * 128, :], sld)
                E_.wait("vector", sld, E_.c(sld))
                vts(E_, t, t, self.flags[:, 2:3], op0=ALU.mult)
                vstt(E_, t, self.p0_tok2[:, :], self.flags[:, 3:4], t, ALU.mult, ALU.add)
            else:
                E_.wait("vector", sld, E_.c(sld))
            ssum, ssq, m, vv = self.st_ssum, self.st_ssq, self.st_m, self.st_v
            rr, negm, rm = self.st_r, self.st_negm, self.st_rm
            vreduce(E_, ssum, t, "sum")
            vstt(E_, self.p0_sq[:, :], t, 1.0, t, ALU.mult, ALU.mult, accum=ssq)
            vts(E_, m, ssum, 1.0 / C)
            vts(E_, vv, ssq, 1.0 / C)
            vtt(E_, self.st_m2, m, m, ALU.mult, sem=sve)
            E_.wait("vector", sve, E_.c(sve))
            vtt(E_, vv, vv, self.st_m2, ALU.subtract, sem=sve)
            E_.wait("scalar", sve, E_.c(sve))
            act(E_, self.st_std, vv, AF.Sqrt, bias=self.epscol[:, 0:1], sem=ssc)
            E_.wait("vector", ssc, E_.c(ssc))
            E_.emit("vector", lambda e, s, o=rr, i_=self.st_std: e.reciprocal(out=o, in_=i_))
            vts(E_, negm, m, -1.0, sem=sve)
            E_.wait("vector", sve, E_.c(sve))
            vtt(E_, rm, rr, m, ALU.mult)
            xt = xn_tok[:, (i % 2) * C:(i % 2) * C + C]
            vts(E_, xt, t, negm[:, 0:1], rr[:, 0:1], op0=ALU.add, op1=ALU.mult, sem=sve)
            tok_free_ve = E_.c(sve)
            xt_ready = E_.c(sve)
            bank = self.pst[i % 2]
            E_.wait("tensor", sve, xt_ready)
            if i >= 2:
                E_.wait("tensor", strv, bank_free[i % 2])
                if build_gates:
                    E_.wait("tensor", "gtv", i - 1)
            for cc in range(NCH):
                transpose(E_, bank[:, cc * 128:(cc + 1) * 128], xt[:, cc * 128:(cc + 1) * 128],
                          self.idb[:, :], sem=spe)
            E_.wait("vector", spe, E_.c(spe))
            toff = dst_tile_off + i * 128
            dst3 = ap3(dst_xn_buf, lead + toff, [[xn_width, NCH], [1, 128]])
            vcopy(E_, dst3, bank[:, :].rearrange("p (cc t) -> p cc t", cc=NCH), sem=strv)
            bank_free[i % 2] = E_.c(strv)
            if build_xf:
                for half in range(2):
                    pb = self.psf[half]
                    if i >= 1:
                        E_.wait("tensor", sxf, self.xf_psf_free[half])
                    for q in range(4):
                        cc = half * 4 + q
                        transpose(E_, pb[:, q * 128:(q + 1) * 128],
                                  t[:, cc * 128:(cc + 1) * 128], self.idf[:, :], sem=spe)
                    tok_free_pe = E_.c(spe)
                    E_.wait("vector", spe, E_.c(spe))
                    xf3 = ap3(self.xf, half * 4 * TOK + i * 128, [[TOK, 4], [1, 128]])
                    vcopy(E_, xf3, pb[:, 0:512].rearrange("p (cc t) -> p cc t", cc=4), sem=sxf)
                    self.xf_psf_free[half] = E_.c(sxf)
                self.xf_ready.append(E_.c(sxf))
            if build_gates:
                self.emit_gates_tile(i, rr, rm)
        self.tok_guard = [(sve, tok_free_ve)]
        if build_xf:
            self.tok_guard.append((spe, tok_free_pe))

    def prep_gating(self):
        E_, d = self.E, self.dram
        self.wgt_f32 = self.smslot(64, "wgt")
        self.wsum_bc = self.smslot(8, "wsum_bc")
        ones_col = self.smslot(1, "ones_f32col")
        memset(E_, "vector", ones_col, 1.0, sem="gve")
        E_.dma("gpsimd", self.tok[0:8, :], d["m_Wg"], "cl")
        E_.wait("tensor", "cl", E_.c("cl"))
        ps = self.psf[2]
        for cc in range(NCH):
            transpose(E_, ps[0:128, cc * 8:(cc + 1) * 8], self.tok[0:8, cc * 128:(cc + 1) * 128],
                      self.idf[0:8, 0:8], sem="gpe")
        self.tok_guard = [("gpe", E_.c("gpe"))]
        E_.wait("vector", "gpe", E_.c("gpe"))
        vcopy(E_, self.wgt_f32, ps[:, 0:64], sem="gve")
        E_.wait("tensor", "gve", E_.c("gve"))
        ps2 = self.psf[3]
        for cc in range(NCH):
            mm(E_, ps2[0:8, 0:1], self.wgt_f32[:, cc * 8:(cc + 1) * 8], ones_col,
               start=(cc == 0), stop=(cc == NCH - 1), sem=("gpe" if cc == NCH - 1 else None))
        E_.wait("vector", "gpe", E_.c("gpe"))
        wsum_col = self.smslot(1, "wsum_col")
        vcopy(E_, wsum_col[0:8, :], ps2[0:8, 0:1], sem="gve")
        E_.wait("tensor", "gve", E_.c("gve"))
        transpose(E_, ps[0:1, 64:72], wsum_col[0:8, :], self.idf[0:8, 0:8], sem="gpe")
        E_.wait("vector", "gpe", E_.c("gpe"))
        vcopy(E_, self.row[0:1, 0:8], ps[0:1, 64:72], sem="gve")
        ones_row = self.row[0:1, 8:136]
        memset(E_, "vector", ones_row, 1.0, sem="gve")
        E_.wait("tensor", "gve", E_.c("gve"))
        mm(E_, ps2[0:128, 8:16], ones_row, self.row[0:1, 0:8], start=True, stop=True, sem="gpe")
        E_.wait("vector", "gpe", E_.c("gpe"))
        vcopy(E_, self.wsum_bc, ps2[:, 8:16], sem="gve")
        s = self.smslot
        self.g_t1 = s(8, "g_t1"); self.g_sco = s(8, "g_sco"); self.g_sig = s(8, "g_sig")
        self.g_m1 = s(1); self.g_m2 = s(1); self.g_e1 = s(8); self.g_ssum = s(1)

    def emit_gates_tile(self, i, rr, rm):
        E_ = self.E
        ps_g = self.psf[4][0:128, (i % 2) * 8:(i % 2) * 8 + 8]
        E_.wait("tensor", self.stream_tag + "x", self.xf_ready[i])
        if i >= 1:
            E_.wait("tensor", "gpr", i)
        for cc in range(NCH):
            mm(E_, ps_g, self.xf[:, cc * TOK + i * 128: cc * TOK + (i + 1) * 128],
               self.wgt_f32[:, cc * 8:(cc + 1) * 8], start=(cc == 0), stop=(cc == NCH - 1),
               sem=("gpe" if cc == NCH - 1 else None))
        E_.wait("vector", "gpe", E_.c("gpe"))
        sco, t1, sig = self.g_sco, self.g_t1, self.g_sig
        m1, m2, e1, ssum = self.g_m1, self.g_m2, self.g_e1, self.g_ssum

        def bar():
            E_.wait("vector", "gve", E_.c("gve"))
        vts(E_, t1, ps_g, rr[:, 0:1], op0=ALU.mult, sem="gpr")
        vts(E_, sco, self.wsum_bc, rm[:, 0:1], op0=ALU.mult, sem="gve")
        bar()
        vtt(E_, sco, t1, sco, ALU.subtract, sem="gve")
        bar()
        vreduce(E_, m1, sco, "max", sem="gve")
        bar()
        vts(E_, e1, sco, m1[:, 0:1], op0=ALU.is_ge, sem="gve")
        bar()
        vstt(E_, t1, e1, -BIGM, sco, ALU.mult, ALU.add, sem="gve")
        bar()
        vreduce(E_, m2, t1, "max", sem="gve")
        bar()
        vtt(E_, ssum, m1, m2, ALU.add, sem="gve")
        bar()
        vstt(E_, t1, sco, 2.0, bcast_free(ssum, 8), ALU.mult, ALU.subtract, sem="gve")
        E_.wait("scalar", "gve", E_.c("gve"))
        act(E_, sig, t1, AF.Sigmoid, sem="gsc")
        E_.wait("vector", "gsc", E_.c("gsc"))
        vts(E_, e1, sco, m2[:, 0:1], op0=ALU.is_ge, sem="gve")
        bar()
        gb = self.p0_gb[:, (i % 2) * 8:(i % 2) * 8 + 8]
        if i >= 2:
            E_.wait("vector", "gtp", i - 1)
        vtt(E_, gb, sig, e1, ALU.mult, sem="gve")
        E_.wait("tensor", "gve", E_.c("gve"))
        ps = self.pst[i % 2]
        E_.wait("tensor", self.stream_tag + "t", E_.c(self.stream_tag + "t"))
        transpose(E_, ps[0:8, 0:128], gb, self.idb[:, :], sem="gtp")
        E_.wait("vector", "gtp", E_.c("gtp"))
        vcopy(E_, self.gatesT[0:8, i * 128:(i + 1) * 128], ps[0:8, 0:128], sem="gtv")

    def p0(self):
        E_, d = self.E, self.dram
        self.arb.mark("zoneA")
        self.xn_oth = self.arb.alloc(NCH * TOK, "xn_oth")
        self.arb.mark("zoneB")
        self.setup_stream_transients()
        if self.mode in ("l0", "fused"):
            self.prep_gating()
        self.xf_psf_free = [0, 0]
        self.stream_tag = "a"
        self.emit_stream("a", d["x_own"], self.xn, TOK + 1, 1, NTILE,
                         build_xf=True, build_gates=(self.mode in ("l0", "fused")))
        if self.mode != "l1":
            self.stream_tag = "b"
            self.emit_stream("b", d["x_oth"], self.xn_oth, TOK, 0, NTILE)

    def emit_dump(self, name, view):
        E_ = self.E
        E_.dma("gpsimd", self.dram[name], view, "dump")

    def finish(self, extra_waits=()):
        E_ = self.E
        for sem in ("dump", "st", *extra_waits):
            if sem in E_.count and E_.count[sem] > 0:
                E_.wait("gpsimd", sem, E_.count[sem])

    def build(self, phases):
        with contextlib.ExitStack() as stack:
            self.alloc(stack)
            self.load_consts()
            self.barrier_consts()
            for ph in phases:
                getattr(self, ph)() if isinstance(ph, str) else ph()
            self.E.finalize()
        return self.nc


def host_constants(core_id):
    parity = core_id % 2
    own_lo = parity * TOK
    ident_bf = np.eye(128, dtype=BF)
    ident_f32 = np.eye(128, dtype=np.float32)
    f = np.zeros((128, 4), np.float32)
    f[:, 0] = 1.0 - parity
    f[:, 1] = float(parity)
    f[:, 2] = float(parity)
    f[:, 3] = 1.0 - parity
    pos = np.concatenate([np.arange(own_lo, own_lo + TOK),
                          np.arange((1 - parity) * TOK, (1 - parity) * TOK + TOK)]).astype(np.float64)
    half = 32
    theta = 10000.0 ** (-np.arange(half, dtype=np.float64) / half)
    freqs = pos[:, None] * theta[None, :]
    cos_t = np.ones((FULL, HD), np.float64)
    cos_t[:, :32] = np.cos(freqs)
    sin_t = np.zeros((FULL, HD), np.float64)
    sin_t[:, :16] = -np.sin(freqs[:, :16])
    sin_t[:, 16:32] = np.sin(freqs[:, 16:32])
    dm = np.zeros((128, 4, TB), BF)
    for d in range(4):
        k_local = d * 128 + np.arange(128)
        q_local = np.arange(TB)
        dm[:, d, :] = (k_local[:, None] <= q_local[None, :]).astype(BF)
    return dict(
        ident_bf=ident_bf, ident_f32=ident_f32, flags=f,
        rope_cos=cos_t.astype(np.float32), rope_sin=sin_t.astype(np.float32),
        diag_mask=np.ascontiguousarray(dm.reshape(128, 4 * TB)),
    )


class AttnMixin:

    def attn_carve(self):
        arb = self.arb
        arb.reset("zoneB")
        self.obuf = arb.alloc(NPAIR * TOK, "obuf")
        self.kbuf = arb.alloc(FULL, "kbuf")
        self.vbuf = arb.alloc(NKT * 130, "vbuf")
        self.qbuf = arb.alloc(TOK, "qbuf")
        self.sqscr = arb.alloc(TB, "sqscr")
        self.wqkv = arb.alloc(2 * C, "wqkv")
        self.pbuf = arb.alloc(2 * TB, "pbuf")
        self.diagb = arb.alloc(4 * TB, "diagb")
        self.ostage = arb.alloc(TB, "ostage")
        self.prod_bf = arb.alloc(1, "prod_bf")
        self.rrow_bf = arb.alloc(TB, "rrow")
        if not hasattr(self, "bias_sl"):
            self.bias_sl = {}
        s = self.smslot
        for nm in ("bq", "bk", "bo"):
            self.bias_sl[nm] = s(8, nm)
        self.negM = s(32, "negM")
        self.negBIG = s(1, "negBIG")
        self.boeff = s(1, "boeff")
        self.n4 = s(4, "n4")
        self.nq = s(2, "nq")
        self.onesf_row = self.row[0:1, 136:264]
        self.nq2 = s(1, "nq2")

    def emit_attn(self, layer):
        E_, d = self.E, self.dram
        nc = self.nc
        pre = "l0_" if layer == 0 else "l1_"
        E_.dma("gpsimd", self.diagb, d["diag"], "cl")
        memset(E_, "vector", self.negBIG, -BIGM, sem="cst")
        for nm, key in (("bq", pre + "bq"), ("bk", pre + "bk"), ("bo", pre + "bo")):
            E_.dma("gpsimd", self.bias_sl[nm], d[key].rearrange("(cc p) -> p cc", p=128), "cl",
                   allow_slow_non_contiguous=True)
        self.onesb = self.diagb[:, 3 * TB + 511: 3 * TB + 512]
        self.bvb = self.pbuf[:, 0:8]
        for eng in ("tensor", "vector", "scalar"):
            E_.wait(eng, "cl", E_.c("cl"))
            E_.wait(eng, "cst", E_.c("cst"))

        xn_own = self.xn
        xn_oth = self.xn_oth
        wq = self.wqkv[:, 0:C]
        wk = self.wqkv[:, C:2 * C]
        wv = wq
        pair_done = 0
        for p in range(NPAIR):
            E_.wait("gpsimd", "pa", pair_done)
            for wsb, key in ((wq, pre + "Wq"), (wk, pre + "Wk")):
                src = d[key][0:C, p * 128:(p + 1) * 128].rearrange("(ch cl) o -> cl ch o", cl=128)
                E_.dma("gpsimd", wsb.rearrange("q (ch o) -> q ch o", ch=NCH), src, "wl")
            wl_now = E_.c("wl")
            E_.wait("tensor", "wl", wl_now)
            E_.wait("tensor", "obc", E_.c("obc"))
            E_.wait("tensor", "ov", E_.c("ov"))
            E_.wait("tensor", "nv", E_.c("nv"))
            for tb in range(NTB):
                ps = self.psf[4 + tb % 2]
                if tb >= 2:
                    E_.wait("tensor", "qv", self.qv_mark[tb - 2])
                for cc in range(NCH):
                    mm(E_, ps[:, 0:TB], wq[:, cc * 128:(cc + 1) * 128],
                       self.xn_ap(cc, tb * TB, TB), start=(cc == 0), stop=(cc == NCH - 1),
                       sem=("pq" if cc == NCH - 1 else None))
                E_.wait("vector", "pq", E_.c("pq"))
                vts(E_, self.qbuf[:, tb * TB:(tb + 1) * TB], ps[:, 0:TB],
                    self.bias_sl["bq"][:, p:p + 1], op0=ALU.add, sem="qv")
                self.qv_mark[tb] = E_.c("qv")
            E_.wait("gpsimd", "pq", E_.c("pq"))
            srcv = d[pre + "Wv"][0:C, p * 128:(p + 1) * 128].rearrange("(ch cl) o -> cl ch o", cl=128)
            E_.dma("gpsimd", wv.rearrange("q (ch o) -> q ch o", ch=NCH), srcv, "wl")
            wl_v = E_.c("wl")
            for half in range(2):
                srcbuf = (lambda cc, lo, n: self.xn_ap(cc, lo, n)) if half == 0 else \
                         (lambda cc, lo, n: xn_oth[:, cc * TOK + lo: cc * TOK + lo + n])
                for tb in range(NTB):
                    ps = self.psf[4 + tb % 2]
                    E_.wait("tensor", "qv", E_.c("qv") - (1 if tb < 2 and half == 0 else 0)
                            if False else E_.c("qv"))
                    for cc in range(NCH):
                        mm(E_, ps[:, 0:TB], wk[:, cc * 128:(cc + 1) * 128],
                           srcbuf(cc, tb * TB, TB), start=(cc == 0), stop=(cc == NCH - 1),
                           sem=("pq" if cc == NCH - 1 else None))
                    E_.wait("vector", "pq", E_.c("pq"))
                    kcol = half * TOK + tb * TB
                    vts(E_, self.kbuf[:, kcol:kcol + TB], ps[:, 0:TB],
                        self.bias_sl["bk"][:, p:p + 1], op0=ALU.add, sem="qv")
            memset(E_, "vector", self.vbuf, 1.0, sem="qv")
            E_.wait("tensor", "qv", E_.c("qv"))
            E_.wait("tensor", "wl", wl_v)
            for it in range(NKT):
                ps = self.psf[4 + it % 2]
                half, tloc = divmod(it * 128, TOK)
                for cc in range(NCH):
                    if half == 0:
                        lhs = self.xn_ap(cc, tloc, 128)
                    else:
                        lhs = xn_oth[:, cc * TOK + tloc: cc * TOK + tloc + 128]
                    mm(E_, ps[:, 0:128], lhs, wv[:, cc * 128:(cc + 1) * 128],
                       start=(cc == 0), stop=(cc == NCH - 1),
                       sem=("pq" if cc == NCH - 1 else None))
                E_.wait("vector", "pq", E_.c("pq"))
                dst = ap3(self.vbuf, it * 130, [[65, 2], [1, 64]])
                vcopy(E_, dst, ps[:, 0:128].rearrange("t (h e) -> t h e", h=2), sem="qv")
                if it >= 1:
                    E_.wait("tensor", "qv", E_.c("qv") - 1)
            E_.wait("tensor", "qv", E_.c("qv"))
            self.emit_norms(p)
            for hi in range(2):
                self.emit_head(layer, p, hi)
            pair_done = E_.c("pa")
        self.emit_wo(layer)

    def emit_norms(self, p):
        E_ = self.E
        E_.wait("scalar", "ov", E_.c("ov"))
        E_.wait("tensor", "obc", E_.c("obc"))
        E_.wait("tensor", "ov", E_.c("ov"))
        E_.wait("tensor", "nv", E_.c("nv"))
        for hi in range(2):
            rows = slice(hi * 64, hi * 64 + 64)
            for which, buf, ntb in (("q", self.qbuf, NTB), ("k", self.kbuf, 2 * NTB)):
                for tb in range(ntb):
                    E_.wait("scalar", "nv", E_.c("nv"))
                    act(E_, self.sqscr[:, :], buf[:, tb * TB:(tb + 1) * TB], AF.Square, sem="nsq")
                    E_.wait("tensor", "nsq", E_.c("nsq"))
                    E_.wait("tensor", "nv", E_.c("nv"))
                    ps = self.psf[4][0:1, 0:TB]
                    mm(E_, ps, self.onesb[rows, :], self.sqscr[rows, 0:TB],
                       start=True, stop=True, sem="npe")
                    E_.wait("vector", "npe", E_.c("npe"))
                    vreduce(E_, self.n4[:, tb % 4:tb % 4 + 1][0:1, :], ps, "max",
                            sem=("nv" if (tb % 4 == 3 or tb == ntb - 1) else "nv"))
                    if tb % 4 == 3 or tb == ntb - 1:
                        E_.wait("vector", "nv", E_.c("nv"))
                        dstq = self.nq[0:1, 0:1] if which == "q" else self.nq[0:1, 1:2]
                        if tb == 3 or ntb == 4:
                            vreduce(E_, dstq, self.n4[0:1, 0:4], "max", sem="nv")
                        else:
                            vreduce(E_, self.n4[0:1, 0:1], self.n4[0:1, 0:4], "max", sem="nv")
                            E_.wait("vector", "nv", E_.c("nv"))
                            vtt(E_, dstq[0:1, :], dstq[0:1, :], self.n4[0:1, 0:1], ALU.max, sem="nv")
                        E_.wait("vector", "nv", E_.c("nv"))
            prod = self.n4[0:1, 0:1]
            vtt(E_, prod, self.nq[0:1, 0:1], self.nq[0:1, 1:2], ALU.mult, sem="nv")
            E_.wait("scalar", "nv", E_.c("nv"))
            act(E_, self.nq2[0:1, :], prod, AF.Sqrt, sem="nsq")
            E_.wait("vector", "nsq", E_.c("nsq"))
            vts(E_, prod, self.nq2[0:1, :], -0.125, sem="nv")
            E_.wait("vector", "nv", E_.c("nv"))
            vcopy(E_, self.prod_bf[0:1, 0:1], prod, sem="nv")
            E_.wait("tensor", "nv", E_.c("nv"))
            ps = self.psf[5][0:128, 0:1]
            mm(E_, ps, self.onesrow_bf[0:1, :], self.prod_bf[0:1, 0:1], start=True, stop=True, sem="npe")
            E_.wait("vector", "npe", E_.c("npe"))
            base = p * 4 + hi * 2
            vcopy(E_, self.negM[:, base:base + 1], ps, sem="nv")
            E_.wait("vector", "nv", E_.c("nv"))
            vstt(E_, self.negM[:, base + 1:base + 2], self.flags[:, 0:1], -BIGM,
                 self.negM[:, base:base + 1], ALU.mult, ALU.add, sem="nv")
            E_.wait("scalar", "nv", E_.c("nv"))

    def exec_kts(self, qb):
        out = []
        for kt in range(4 * qb + 4):
            diag = kt - 4 * qb if kt >= 4 * qb else None
            out.append((kt, kt * 128, "own", diag))
        for kt in range(16, 32):
            out.append((kt, 2048 + (kt - 16) * 128, "oth", None))
        return out

    def emit_head(self, layer, p, hi):
        E_ = self.E
        rows = slice(hi * 64, hi * 64 + 64)
        base = p * 4 + hi * 2
        negM_own = self.negM[:, base:base + 1]
        negM_oth = self.negM[:, base + 1:base + 2]


        for qb in range(NTB):
            kts = self.exec_kts(qb)
            flip = self.oq_flip % 2
            ops = self.psf[2 + flip]
            if self.oq_flip >= 2 and self.ov_mark.get(flip):
                E_.wait("tensor", "ov", self.ov_mark[flip])
            self.oq_flip += 1
            n = len(kts)
            for idx, (kt, kcol, kind, diagi) in enumerate(kts):
                sps = self.psf[idx % 2]
                E_.wait("tensor", "sx", self.sx_mark[idx - 2] if idx >= 2 else E_.c("sx"))
                if diagi is not None or idx < 2:
                    E_.wait("tensor", "vm", E_.c("vm"))
                mm(E_, sps[:, 0:TB], self.kbuf[rows, kcol:kcol + 128],
                   self.qbuf[rows, qb * TB:(qb + 1) * TB], start=True, stop=True, sem="ps")
                E_.wait("scalar", "ps", E_.c("ps"))
                E_.wait("scalar", "pa", self.pa_mark[idx - 2] if idx >= 2 else E_.c("pa"))
                bias = negM_own if kind == "own" else negM_oth
                pb = self.pbuf[:, (idx % 2) * TB:(idx % 2) * TB + TB]
                act(E_, pb, sps[:, 0:TB], AF.Exp, bias=bias[:, 0:1], scale=0.125, sem="sx")
                self.sx_mark[idx] = E_.c("sx")
                dep = "sx"
                if diagi is not None:
                    E_.wait("vector", "sx", E_.c("sx"))
                    vtt(E_, pb, pb, self.diagb[:, diagi * TB:(diagi + 1) * TB], ALU.mult, sem="vm")
                    dep = "vm"
                E_.wait("tensor", dep, E_.c(dep))
                vo = kt * self.vstride + hi * self.v_head_off
                vslice = self.vbuf[:, vo: vo + 65]
                mm(E_, ops[0:65, 0:TB], vslice, pb, start=(idx == 0), stop=(idx == n - 1),
                   sem="pa")
                self.pa_mark[idx] = E_.c("pa")
            E_.wait("vector", "pa", E_.c("pa"))
            if hi == 1:
                E_.wait("vector", "os", E_.c("os"))
            rr = self.rrow_bf[0:1, 0:TB]
            rrf = self.row[0:1, 0:TB]
            E_.emit("vector", lambda e, s, o=rrf, i_=ops[64:65, 0:TB]: e.reciprocal(out=o, in_=i_),
                    sem="ve2")
            E_.wait("vector", "ve2", E_.c("ve2"))
            vcopy(E_, rr, rrf, sem="ve2")
            E_.wait("tensor", "ve2", E_.c("ve2"))
            E_.wait("tensor", "obc", E_.c("obc"))
            E_.wait("tensor", "nv", E_.c("nv"))
            bc = self.psf[4][0:64, 0:TB]
            mm(E_, bc, self.onesrow_bf[0:1, 0:64], rr, start=True, stop=True, sem="npe")
            E_.wait("scalar", "npe", E_.c("npe"))
            scopy(E_, self.sqscr[0:64, 0:TB], bc, sem="obc")
            E_.wait("vector", "obc", E_.c("obc"))
            if hi == 0:
                odst = self.obuf[0:64, p * TOK + qb * TB: p * TOK + qb * TB + TB]
            else:
                odst = self.ostage[0:64, 0:TB]
            vtt(E_, odst, ops[0:64, 0:TB], self.sqscr[0:64, 0:TB], ALU.mult, sem="ov")
            self.ov_mark[flip] = E_.c("ov")
            if hi == 1:
                E_.wait("sync", "ov", E_.c("ov"))
                E_.dma("sync", self.obuf[64:128, p * TOK + qb * TB: p * TOK + qb * TB + TB],
                       self.ostage[0:64, 0:TB], "os")

    def emit_wo(self, layer):
        E_, d = self.E, self.dram
        pre = "l0_" if layer == 0 else "l1_"
        wo = self.wqkv[:, 0:C]
        E_.wait("gpsimd", "pa", E_.c("pa"))
        E_.wait("gpsimd", "ov", E_.c("ov"))
        if layer == 0:
            E_.dma("gpsimd", self.bvb, d[pre + "bv"].rearrange("(cc p) -> p cc", p=128), "wl",
                   allow_slow_non_contiguous=True)
        E_.wait("sync", "os", E_.c("os"))
        for co in range(NCH):
            E_.wait("gpsimd", "wv2", E_.c("wv2"))
            src = d[pre + "Wo"][0:C, co * 128:(co + 1) * 128].rearrange("(ch cl) o -> cl ch o", cl=128)
            E_.dma("gpsimd", wo.rearrange("q (ch o) -> q ch o", ch=NCH), src, "wl")
            E_.wait("tensor", "wl", E_.c("wl"))
            E_.wait("tensor", "os", E_.c("os"))
            ps1 = self.psf[4][0:128, TB - 1:TB]
            for cc in range(NCH):
                mm(E_, ps1, wo[:, cc * 128:(cc + 1) * 128], self.bvb[:, cc:cc + 1],
                   start=(cc == 0), stop=(cc == NCH - 1), sem=("npe" if cc == NCH - 1 else None))
            E_.wait("vector", "npe", E_.c("npe"))
            vts(E_, self.boeff, ps1, self.bias_sl["bo"][:, co:co + 1], op0=ALU.add, sem="nv")
            E_.wait("scalar", "nv", E_.c("nv"))
            E_.wait("vector", "nv", E_.c("nv"))
            for tb in range(NTB):
                ps = self.psf[5 - tb % 2]
                if co > 0 or tb >= 2:
                    E_.wait("tensor", "wv2", E_.c("wv2") - 1)
                for pp in range(NPAIR):
                    mm(E_, ps[:, 0:TB], wo[:, pp * 128:(pp + 1) * 128],
                       self.obuf[:, pp * TOK + tb * TB: pp * TOK + tb * TB + TB],
                       start=(pp == 0), stop=(pp == NPAIR - 1),
                       sem=("pq" if pp == NPAIR - 1 else None))
                E_.wait("vector", "pq", E_.c("pq"))
                xsl = self.xf[:, co * TOK + tb * TB: co * TOK + tb * TB + TB]
                vstt(E_, xsl, ps[:, 0:TB], self.boeff[:, 0:1], xsl, ALU.add, ALU.add, sem="wv2")

    def phase_barrier(self, engs=("tensor", "vector", "scalar", "gpsimd", "sync")):
        for eng in engs:
            for sem in list(self.E.count):
                if self.E.count[sem] > 0:
                    self.E.wait(eng, sem, self.E.count[sem])

    def attn_l0(self):
        self.oq_flip = 0
        self.sx_mark = {}
        self.pa_mark = {}
        self.qv_mark = {}
        self.ov_mark = {}
        self.vstride = 130
        self.v_head_off = 65
        self.attn_carve()
        self.phase_barrier(("tensor",))
        self.emit_attn(0)


for _nm in ("attn_carve", "emit_attn", "emit_norms", "exec_kts", "emit_head", "emit_wo", "attn_l0", "phase_barrier"):
    setattr(MKProgram, _nm, getattr(AttnMixin, _nm))


class MoEMixin:
    def emit_casts(self):
        E_, d = self.E, self.dram
        self.cast_mark = {}
        if self.mode not in ("l0", "fused"):
            for mat, src_, dst in (("wr", "r_Wr", "wrs"), ("wk", "r_Wk", "wks"),
                                   ("wv", "r_Wv", "wvs")):
                E_.dma("gpsimd", d[dst], d[src_], "cast")
                self.cast_mark[mat] = E_.c("cast")
            return
        for e in range(E):
            for mat, src, dst in (("w1", "m_W1", "w1s"), ("w3", "m_W3", "w3s"),
                                  ("w2", "m_W2", "w2s")):
                E_.dma("gpsimd", d[dst][e], d[src][e], "cast")
                self.cast_mark[(mat, e)] = E_.c("cast")
        if self.mode in ("l1", "fused"):
            for mat, src, dst in (("wr", "r_Wr", "wrs"), ("wk", "r_Wk", "wks"),
                                  ("wv", "r_Wv", "wvs")):
                E_.dma("gpsimd", d[dst], d[src], "cast")
                self.cast_mark[mat] = E_.c("cast")

    def moe_carve(self):
        arb = self.arb
        arb.reset("zoneA")
        self.g = arb.alloc(NHC * 1024, "g")
        self.w1t = arb.alloc(2 * 4096, "w1t")
        self.w3t = arb.alloc(2 * 4096, "w3t")
        self.w2t = arb.alloc(2 * 384, "w2t")
        self.a1t = arb.alloc(128, "a1t")
        self.a3t = arb.alloc(128, "a3t")
        self.a2t = arb.alloc(256, "a2t")
        self.b1t = arb.alloc(2048, "b1t")
        self.b3t = arb.alloc(2048, "b3t")
        self.b2t = arb.alloc(1024, "b2t")
        self.u1 = arb.alloc(1024, "u1")
        self.u3 = arb.alloc(1024, "u3")
        self.u2 = arb.alloc(1024, "u2")
        self.gbc = arb.alloc(TOK, "gbc")
        self.anat = arb.alloc(1024, "anat")
        self.bnat = arb.alloc(256, "bnat")
        self.b2nat = arb.alloc(1024, "b2nat")
        self.silu_sb = arb.alloc(2 * TB, "silu")
        self.h3g = arb.alloc(2 * TB, "h3g")
        s = self.smslot
        self.b1col = s(16, "b1col")
        self.b3col = s(16, "b3col")

    def moe_lora_prep(self, e):
        E_, d = self.E, self.dram
        E_.wait("gpsimd", "mp", E_.c("mp"))
        E_.wait("gpsimd", "mv", E_.c("mv"))
        E_.dma("gpsimd", self.anat[0:16, :], d["m_A1"][e], "mc")
        a1_ld = E_.c("mc")
        def a_transpose(dst, ncc, ld_mark):
            E_.wait("tensor", "mc", ld_mark)
            E_.wait("tensor", "mtv", E_.c("mtv"))
            bank = self.pst[1]
            for cc in range(ncc):
                transpose(E_, bank[0:128, cc * 16:(cc + 1) * 16],
                          self.anat[0:16, cc * 128:(cc + 1) * 128],
                          self.idb[0:16, 0:16], sem="mt")
            E_.wait("vector", "mt", E_.c("mt"))
            vcopy(E_, dst, bank[0:128, 0:ncc * 16], sem="mtv")
        a_transpose(self.a1t, 8, a1_ld)
        E_.wait("gpsimd", "mtv", E_.c("mtv"))
        E_.dma("gpsimd", self.anat[0:16, :], d["m_A3"][e], "mc")
        a_transpose(self.a3t, 8, E_.c("mc"))
        E_.wait("gpsimd", "mtv", E_.c("mtv"))
        E_.dma("gpsimd", self.anat[0:16, 0:1024], d["m_A2"][e][0:16, 0:1024], "mc")
        a_transpose(self.a2t[:, 0:128], 8, E_.c("mc"))
        E_.wait("gpsimd", "mtv", E_.c("mtv"))
        E_.dma("gpsimd", self.anat[0:16, 0:1024], d["m_A2"][e][0:16, 1024:2048], "mc")
        a_transpose(self.a2t[:, 128:256], 8, E_.c("mc"))
        def b_transpose(dst, src_ap, ntile):
            E_.wait("gpsimd", "mtv", E_.c("mtv"))
            src3 = bass.AP(src_ap.tensor, src_ap.offset,
                           [[16, 128], [128 * 16, ntile], [1, 16]])
            E_.dma("gpsimd", self.bnat[:, 0:ntile * 16].rearrange("p (t r) -> p t r", r=16),
                   src3, "mc")
            E_.wait("tensor", "mc", E_.c("mc"))
            E_.wait("tensor", "mtv", E_.c("mtv"))
            for t in range(ntile):
                bank = self.pst[(t // 8) % 2]
                transpose(E_, bank[0:16, (t % 8) * 128:(t % 8 + 1) * 128],
                          self.bnat[:, t * 16:(t + 1) * 16], self.idb[:, :], sem="mt")
                if t % 8 == 7 or t == ntile - 1:
                    E_.wait("vector", "mt", E_.c("mt"))
                    lo = (t // 8) * 8
                    vcopy(E_, dst[0:16, lo * 128:(t + 1) * 128],
                          bank[0:16, 0:(t - lo + 1) * 128], sem="mtv")
        b_transpose(self.b1t, d["m_B1"][e], 16)
        b_transpose(self.b3t, d["m_B3"][e], 16)
        b_transpose(self.b2t, d["m_B2"][e], 8)
        E_.wait("gpsimd", "ms", E_.c("ms"))
        E_.wait("gpsimd", "mv", E_.c("mv"))
        E_.dma("gpsimd", self.b1col, d["m_b1"][e].rearrange("(hc p) -> p hc", p=128), "mc",
               allow_slow_non_contiguous=True)
        E_.dma("gpsimd", self.b3col, d["m_b3"][e].rearrange("(hc p) -> p hc", p=128), "mc",
               allow_slow_non_contiguous=True)
        E_.wait("sync", "mbc", E_.c("mbc"))
        E_.wait("sync", "mv", E_.c("mv"))
        grow = self.gbc[0:1, :]
        E_.dma("sync", grow, self.gatesT[e:e + 1, :], "mg")
        E_.wait("tensor", "mg", E_.c("mg"))
        for tb in range(NTB):
            mm(E_, self.psf[tb % 2][0:128, 0:TB], self.onesrow_bf[0:1, :],
               grow[0:1, tb * TB:(tb + 1) * TB], start=True, stop=True, sem="mt")
            E_.wait("scalar", "mt", E_.c("mt"))
            scopy(E_, self.gbc[:, tb * TB:(tb + 1) * TB], self.psf[tb % 2][0:128, 0:TB],
                  sem="mbc")
            if tb >= 1:
                E_.wait("tensor", "mbc", E_.c("mbc") - 1)
        self.mc_mark = E_.c("mc")

    def moe_u13(self, e, half):
        E_ = self.E
        for which, at, bdst in (("u1", self.a1t, self.u1), ("u3", self.a3t, self.u3)):
            for tb2 in range(2):
                tb = half * 2 + tb2
                ps = self.psf[4][0:16, 0:TB]
                E_.wait("tensor", "muv", E_.c("muv"))
                for cc in range(NCH):
                    mm(E_, ps, at[:, cc * 16:(cc + 1) * 16],
                       self.xn_ap(cc, tb * TB, TB), start=(cc == 0), stop=(cc == NCH - 1),
                       sem=("mup" if cc == NCH - 1 else None))
                E_.wait("vector", "mup", E_.c("mup"))
                vcopy(E_, bdst[0:16, tb2 * TB:(tb2 + 1) * TB], ps, sem="muv")

    def moe_phase_a(self, e, half):
        E_, d = self.E, self.dram
        flip = 0
        for hg in range(4):
            buf = hg % 2
            E_.wait("sync", "mp", self.w13_free.get(buf, 0))
            for mat, scr, dstb in (("w1", "w1s", self.w1t), ("w3", "w3s", self.w3t)):
                E_.wait("sync", "cast", self.cast_mark[(mat, e)])
                for cc in range(NCH):
                    E_.dma("sync", dstb[:, buf * 4096 + cc * 512:(buf) * 4096 + (cc + 1) * 512],
                           d[scr][e][hg * 512:(hg + 1) * 512, cc * 128:(cc + 1) * 128],
                           "wt", transpose=True)
            wt_mark = E_.c("wt")
            E_.wait("tensor", "wt", wt_mark)
            for hc4 in range(4):
                hc = hg * 4 + hc4
                for tb2 in range(2):
                    tb = half * 2 + tb2
                    pa = self.psf[flip % 2]
                    pb = self.psf[2 + flip % 2]
                    E_.wait("tensor", "ms", self.ms_mark.get(flip % 2, 0))
                    for cc in range(NCH):
                        mm(E_, pa[:, 0:TB],
                           self.w1t[:, buf * 4096 + cc * 512 + hc4 * 128: buf * 4096 + cc * 512 + hc4 * 128 + 128],
                           self.xn_ap(cc, tb * TB, TB), start=(cc == 0), stop=False)
                    mm(E_, pa[:, 0:TB], self.b1t[0:16, hc * 128:(hc + 1) * 128],
                       self.u1[0:16, tb2 * TB:(tb2 + 1) * TB], start=False, stop=True, sem="mp")
                    E_.wait("scalar", "mp", E_.c("mp"))
                    E_.wait("scalar", "mc", self.mc_mark)
                    act(E_, self.silu_sb[:, (flip % 2) * TB:(flip % 2) * TB + TB], pa[:, 0:TB],
                        AF.Silu, bias=self.b1col[:, hc:hc + 1], sem="ms")
                    self.ms_mark[flip % 2] = E_.c("ms")
                    E_.wait("tensor", "mv", self.mv_mark.get(flip % 2, 0))
                    for cc in range(NCH):
                        mm(E_, pb[:, 0:TB],
                           self.w3t[:, buf * 4096 + cc * 512 + hc4 * 128: buf * 4096 + cc * 512 + hc4 * 128 + 128],
                           self.xn_ap(cc, tb * TB, TB), start=(cc == 0), stop=False)
                    mm(E_, pb[:, 0:TB], self.b3t[0:16, hc * 128:(hc + 1) * 128],
                       self.u3[0:16, tb2 * TB:(tb2 + 1) * TB], start=False, stop=True, sem="mp")
                    E_.wait("vector", "mp", E_.c("mp"))
                    h3gs = self.h3g[:, (flip % 2) * TB:(flip % 2) * TB + TB]
                    vstt(E_, h3gs, pb[:, 0:TB], self.b3col[:, hc:hc + 1],
                         self.gbc[:, tb * TB:(tb + 1) * TB], ALU.add, ALU.mult, sem="mv")
                    E_.wait("vector", "ms", self.ms_mark[flip % 2])
                    gd = self.g[:, hc * 1024 + tb2 * TB: hc * 1024 + tb2 * TB + TB]
                    vtt(E_, gd, self.silu_sb[:, (flip % 2) * TB:(flip % 2) * TB + TB],
                        h3gs, ALU.mult, sem="mv")
                    self.mv_mark[flip % 2] = E_.c("mv")
                    flip += 1
            self.w13_free[buf] = E_.c("mp")

    def moe_u2(self, e, half):
        E_ = self.E
        for tb2 in range(2):
            ps = self.psf[4][0:16, 0:TB]
            E_.wait("tensor", "muv", E_.c("muv"))
            E_.wait("tensor", "mv", E_.c("mv"))
            for hc in range(NHC):
                mm(E_, ps, self.a2t[:, hc * 16:(hc + 1) * 16],
                   self.g[:, hc * 1024 + tb2 * TB: hc * 1024 + tb2 * TB + TB],
                   start=(hc == 0), stop=(hc == NHC - 1), sem=("mup" if hc == NHC - 1 else None))
            E_.wait("vector", "mup", E_.c("mup"))
            vcopy(E_, self.u2[0:16, tb2 * TB:(tb2 + 1) * TB], ps, sem="muv")

    def moe_phase_b(self, e, half):
        E_, d = self.E, self.dram
        E_.wait("tensor", "ms", E_.c("ms"))
        E_.wait("tensor", "mv", E_.c("mv"))
        E_.wait("tensor", "muv", E_.c("muv"))
        trios = [(0, 3), (3, 6), (6, 8)]
        for (d0, d1) in trios:
            nd = d1 - d0
            for hc in range(NHC):
                buf = hc % 2
                E_.wait("sync", "mp", self.w2_free.get(buf, 0))
                E_.wait("sync", "cast", self.cast_mark[("w2", e)])
                E_.dma("sync", self.w2t[:, buf * 384: buf * 384 + nd * 128],
                       d["w2s"][e][d0 * 128:d1 * 128, hc * 128:(hc + 1) * 128],
                       "wt", transpose=True)
                E_.wait("tensor", "wt", E_.c("wt"))
                if hc == 0:
                    E_.wait("tensor", "mxv", E_.c("mxv"))
                for ci in range(nd):
                    for tb2 in range(2):
                        tb = half * 2 + tb2
                        mm(E_, self.psf[ci * 2 + tb2][:, 0:TB],
                           self.w2t[:, buf * 384 + ci * 128: buf * 384 + (ci + 1) * 128],
                           self.g[:, hc * 1024 + tb2 * TB: hc * 1024 + tb2 * TB + TB],
                           start=(hc == 0), stop=False,
                           sem=("mp" if (ci == nd - 1 and tb2 == 1) else None))
                self.w2_free[buf] = E_.c("mp")
            E_.wait("tensor", "muv", E_.c("muv"))
            if e == E - 1:
                E_.wait("tensor", "mc", self.b2_mark)
            for ci in range(nd):
                dc = d0 + ci
                for tb2 in range(2):
                    tb = half * 2 + tb2
                    fin = (ci == nd - 1 and tb2 == 1)
                    if e == E - 1:
                        mm(E_, self.psf[ci * 2 + tb2][:, 0:TB],
                           self.b2t[0:16, dc * 128:(dc + 1) * 128],
                           self.u2[0:16, tb2 * TB:(tb2 + 1) * TB], start=False, stop=False)
                        mm(E_, self.psf[ci * 2 + tb2][:, 0:TB],
                           self.b2nat[0:8, dc * 128:(dc + 1) * 128],
                           self.gatesT[0:8, tb * TB:(tb + 1) * TB], start=False, stop=True,
                           sem=("mxp" if fin else None))
                    else:
                        mm(E_, self.psf[ci * 2 + tb2][:, 0:TB],
                           self.b2t[0:16, dc * 128:(dc + 1) * 128],
                           self.u2[0:16, tb2 * TB:(tb2 + 1) * TB], start=False, stop=True,
                           sem=("mxp" if fin else None))
            E_.wait("vector", "mxp", E_.c("mxp"))
            for ci in range(nd):
                dc = d0 + ci
                for tb2 in range(2):
                    tb = half * 2 + tb2
                    xsl = self.xf[:, dc * TOK + tb * TB: dc * TOK + tb * TB + TB]
                    vtt(E_, xsl, xsl, self.psf[ci * 2 + tb2][:, 0:TB], ALU.add, sem="mxv")

    def moe(self):
        E_ = self.E
        self.moe_carve()
        self.phase_barrier(("tensor", "vector", "scalar", "sync", "gpsimd"))
        E_.dma("gpsimd", self.b2nat[0:8, :], self.dram["m_b2"], "mc")
        self.b2_mark = E_.c("mc")
        self.w13_free = {}
        self.w2_free = {}
        self.ms_mark = {}
        self.mv_mark = {}
        import os as _os
        _only = _os.environ.get("MOE_ONLY", "")
        for e in range(E):
            if _only == "prep7" and e != E - 1:
                continue
            self.moe_lora_prep(e)
            if _only == "prep7":
                continue
            for half in range(2):
                self.moe_u13(e, half)
                if _only == "u13":
                    continue
                self.moe_phase_a(e, half)
                if _only == "pa":
                    continue
                self.moe_u2(e, half)
                self.moe_phase_b(e, half)


for _nm in ("emit_casts", "moe_carve", "moe_lora_prep", "moe_u13", "moe_phase_a",
            "moe_u2", "moe_phase_b", "moe"):
    setattr(MKProgram, _nm, getattr(MoEMixin, _nm))


class L1Mixin:
    def emit_stage_out(self, dst_rows):
        E_ = self.E
        self.phase_barrier(("tensor",))
        for i in range(NTILE):
            for halfc in range(2):
                pb = self.psf[halfc]
                if i >= 1:
                    E_.wait("tensor", "sgv", self.sg_free[halfc])
                for q in range(4):
                    cc = halfc * 4 + q
                    transpose(E_, pb[:, q * 128:(q + 1) * 128],
                              self.xf[:, cc * TOK + i * 128: cc * TOK + (i + 1) * 128],
                              self.idf[:, :], sem="sg")
                E_.wait("vector", "sg", E_.c("sg"))
                if i >= 1:
                    E_.wait("vector", "stg", 16 * i)
                vcopy(E_, self.tok[:, halfc * 512:(halfc + 1) * 512].rearrange(
                    "p (cc t) -> p cc t", cc=4),
                    pb[:, 0:512].rearrange("p (cc t) -> p cc t", cc=4), sem="sgv")
                self.sg_free[halfc] = E_.c("sgv")
            E_.wait("gpsimd", "sgv", E_.c("sgv"))
            E_.dma("gpsimd", dst_rows[i * 128:(i + 1) * 128, :], self.tok[:, :], "stg")

    def stage_out_final(self):
        self.sg_free = [0, 0]
        self.emit_stage_out(self.dram["out"])

    def l1_stage_ag(self):
        E_ = self.E
        self.sg_free = [0, 0]
        self.emit_stage_out(self.dram["cc_in"])
        E_.wait("gpsimd", "stg", E_.c("stg"))
        E_.emit("gpsimd", lambda e, s: e.collective_compute(
            "AllGather", mybir.AluOpType.bypass,
            replica_groups=[[0, 1], [2, 3], [4, 5], [6, 7]],
            ins=[self.dram["cc_in"].opt()], outs=[self.dram["cc_out"].opt()]),
            sem="cc", inc=1)

    def l1_ln(self):
        E_ = self.E
        self.phase_barrier(("tensor", "vector", "scalar"))
        onescol_f = self.smslot(1, "onescol_f")
        memset(E_, "vector", onescol_f, 1.0, sem="lnv")
        arb = self.arb
        arb.reset("zoneA")
        self.ab_bf = arb.alloc(1024, "ab_bf")
        self.AB_sb = arb.alloc(1024, "AB_sb")
        a_bf = self.ab_bf[0:1, 0:TB]
        b_bf = self.ab_bf[0:1, TB:2 * TB]
        A_sb = self.AB_sb[:, 0:TB]
        B_sb = self.AB_sb[:, TB:2 * TB]
        m_row = self.row[0:1, 0:TB]
        E_.wait("tensor", "lnv", E_.c("lnv"))
        for tb in range(NTB):
            ps_s = self.psf[0][0:1, 0:TB]
            ps_q = self.psf[1][0:1, 0:TB]
            ps_t = self.psf[2][0:1, 0:TB]
            E_.wait("tensor", "lnv", E_.c("lnv"))
            E_.wait("tensor", "lns", E_.c("lns"))
            E_.wait("scalar", "lnv", E_.c("lnv"))
            for cc in range(NCH):
                mm(E_, ps_s, onescol_f, self.xf[:, cc * TOK + tb * TB: cc * TOK + (tb + 1) * TB],
                   start=(cc == 0), stop=(cc == NCH - 1), sem=("lnp" if cc == NCH - 1 else None))
            for cc in range(NCH):
                if cc > 0 or tb > 0:
                    E_.wait("scalar", "lnq2", E_.c("lnq2"))
                act(E_, self.tok[:, 0:TB], self.xf[:, cc * TOK + tb * TB: cc * TOK + (tb + 1) * TB],
                    AF.Square, sem="lnq")
                E_.wait("tensor", "lnq", E_.c("lnq"))
                mm(E_, ps_q, onescol_f, self.tok[:, 0:TB],
                   start=(cc == 0), stop=(cc == NCH - 1),
                   sem="lnq2", inc=1)
            E_.wait("vector", "lnq2", E_.c("lnq2"))
            E_.wait("vector", "lnp", E_.c("lnp"))
            def bar(sem="lnv"):
                E_.wait("vector", sem, E_.c(sem))
            E_.wait("vector", "lns", E_.c("lns"))
            vcopy(E_, m_row, ps_s, sem="lnv")
            bar()
            vts(E_, m_row, m_row, 1.0 / C, sem="lnv")
            bar()
            E_.wait("scalar", "lnv", E_.c("lnv"))
            act(E_, a_bf, m_row, AF.Square, sem="lns")
            E_.wait("vector", "lns", E_.c("lns"))
            vstt(E_, ps_q, ps_q, 1.0 / C, a_bf, ALU.mult, ALU.subtract, sem="lnv")
            bar()
            E_.wait("scalar", "lnv", E_.c("lnv"))
            act(E_, ps_q, ps_q, AF.Sqrt, bias=self.epscol[0:1, 0:1], sem="lns")
            E_.wait("vector", "lns", E_.c("lns"))
            E_.emit("vector", lambda e, s, o=ps_t, i_=ps_q: e.reciprocal(out=o, in_=i_),
                    sem="lnv")
            bar()
            vcopy(E_, a_bf, ps_t, sem="lnv")
            bar()
            vstt(E_, b_bf, ps_t, -1.0, m_row, ALU.mult, ALU.mult, sem="lnv")
            bar()
            E_.wait("tensor", "lnv", E_.c("lnv"))
            mm(E_, self.psf[3][0:128, 0:TB], self.onesrow_bf[0:1, :], a_bf,
               start=True, stop=True, sem="lnp")
            mm(E_, self.psf[4][0:128, 0:TB], self.onesrow_bf[0:1, :], b_bf,
               start=True, stop=True, sem="lnp")
            E_.wait("scalar", "lnp", E_.c("lnp"))
            scopy(E_, A_sb, self.psf[3][0:128, 0:TB], sem="lns")
            scopy(E_, B_sb, self.psf[4][0:128, 0:TB], sem="lns")
            E_.wait("vector", "lns", E_.c("lns"))
            for cc in range(NCH):
                xsl = self.xf[:, cc * TOK + tb * TB: cc * TOK + (tb + 1) * TB]
                vtt(E_, self.tok[:, 0:TB], xsl, A_sb, ALU.mult)
                vtt(E_, self.xn_ap(cc, tb * TB, TB), self.tok[:, 0:TB], B_sb, ALU.add,
                    sem="lnv")

    def l1_carve(self):
        arb = self.arb
        arb.reset("zoneA")
        self.k2 = arb.alloc(FULL, "k2")
        self.vb1 = arb.alloc(NKT * 65 + 63, "vb1")
        self.tcos = arb.alloc(NKT * 64, "tcos")
        self.tsin = arb.alloc(NKT * 64, "tsin")
        self.bkrow = arb.alloc(64, "bkrow")
        self.bqrow = arb.alloc(C, "bqrow")
        arb.mark("zoneE")
        s = self.smslot
        self.mur = s(8, "mur"); self.muk = s(8, "muk")
        self.murm = s(8, "murm"); self.mukm = s(8, "mukm")

    def l1_stream(self):
        E_, d = self.E, self.dram
        arb = self.arb
        arb.reset("zoneE")
        self.xn_oth = arb.alloc(NCH * TOK, "xn1_oth")
        self.setup_stream_transients()
        self.stream_tag = "c"
        if self.mode == "fused":
            E_.wait("gpsimd", "cc", 1)
            self.emit_stream("c", d["cc_out"][0:TOK, :], self.xn_oth, TOK, 0, NTILE,
                             src2_rows=d["cc_out"][TOK:FULL, :])
        else:
            self.emit_stream("c", d["x_oth"], self.xn_oth, TOK, 0, NTILE)
        for (gs, gv) in self.tok_guard:
            E_.wait("vector", gs, gv)
        for cc in range(NCH):
            vts(E_, self.xn[:, cc * (TOK + 1): cc * (TOK + 1) + 1],
                self.xn_oth[:, cc * TOK + TOK - 1: cc * TOK + TOK],
                self.flags[:, 1:2], op0=ALU.mult, sem="bnd")

    def l1_kv(self):
        E_, d = self.E, self.dram
        E_.dma("gpsimd", self.tcos.rearrange("p (t d) -> p t d", d=64),
               bass.AP(d["rope_cos"].tensor, d["rope_cos"].offset,
                       [[64, 128], [128 * 64, NKT], [1, 64]]), "kl")
        E_.dma("gpsimd", self.tsin.rearrange("p (t d) -> p t d", d=64),
               bass.AP(d["rope_sin"].tensor, d["rope_sin"].offset,
                       [[64, 128], [128 * 64, NKT], [1, 64]]), "kl")
        E_.dma("gpsimd", self.bkrow[0:1, :], d["l1_bk"].rearrange("d -> 1 d" if False else "(one d) -> one d", one=1), "kl")
        E_.dma("gpsimd", self.bqrow[0:1, :], d["l1_bq"].rearrange("(one d) -> one d", one=1), "kl")
        self.wkv_sb = self.arb.alloc(1024, "wkv")
        E_.dma("gpsimd", ap3(self.wkv_sb, 0, [[128, NCH], [1, 64]]),
               d["l1_Wk"][0:C, 0:HD].rearrange("(ch cl) o -> cl ch o", cl=128), "kl")
        E_.dma("gpsimd", ap3(self.wkv_sb, 64, [[128, NCH], [1, 64]]),
               d["l1_Wv"][0:C, 0:HD].rearrange("(ch cl) o -> cl ch o", cl=128), "kl")
        E_.wait("tensor", "kl", E_.c("kl"))
        E_.wait("vector", "kl", E_.c("kl"))
        memset(E_, "vector", self.vb1, 1.0, sem="kv")
        E_.wait("tensor", "kv", E_.c("kv"))
        ropescr = self.tok[:, 0:128]
        ktb = self.pbuf_kv = self.arb.alloc(64, "ktb")
        for it in range(NKT):
            half, tloc = divmod(it * 128, TOK)
            ps = self.psf[4 + it % 2]
            if it >= 2:
                E_.wait("tensor", "kv", self.kv_mark[it - 2])
            for cc in range(NCH):
                lhs = self.xn_ap(cc, tloc, 128) if half == 0 else \
                    self.xn_oth[:, cc * TOK + tloc: cc * TOK + tloc + 128]
                mm(E_, ps[:, 0:128], lhs, self.wkv_sb[:, cc * 128:(cc + 1) * 128],
                   start=(cc == 0), stop=False)
            mm(E_, ps[:, 0:64], self.onesrow_bf[0:1, :], self.bkrow[0:1, 0:64],
               start=False, stop=True, sem="kp")
            E_.wait("vector", "kp", E_.c("kp"))
            tc = self.tcos[:, it * 64:(it + 1) * 64]
            ts_ = self.tsin[:, it * 64:(it + 1) * 64]
            vtt(E_, ropescr[:, 0:64], ps[:, 0:64], tc, ALU.mult)
            vtt(E_, ropescr[:, 64:80], ps[:, 16:32], ts_[:, 0:16], ALU.mult)
            vtt(E_, ropescr[:, 80:96], ps[:, 0:16], ts_[:, 16:32], ALU.mult, sem="kv2")
            E_.wait("vector", "kv2", E_.c("kv2"))
            vtt(E_, ktb[:, 0:32], ropescr[:, 0:32], ropescr[:, 64:96], ALU.add)
            vts(E_, ktb[:, 32:64], ropescr[:, 32:64], 1.0, sem="kv2")
            E_.wait("vector", "kv2", E_.c("kv2"))
            vcopy(E_, self.vb1[:, it * 65: it * 65 + 64], ps[:, 64:128], sem="kv")
            self.kv_mark[it] = E_.c("kv")
            E_.wait("tensor", "kv2", E_.c("kv2"))
            bank = self.pst[it % 2]
            if it >= 2:
                E_.wait("tensor", "ktv", E_.c("ktv") - 1)
            transpose(E_, bank[0:64, 0:128], ktb[:, 0:64], self.idb[:, :], sem="kp")
            E_.wait("vector", "kp", E_.c("kp"))
            vcopy(E_, self.k2[0:64, it * 128:(it + 1) * 128], bank[0:64, 0:128], sem="ktv")
        E_.wait("sync", "ktv", E_.c("ktv"))
        E_.dma("sync", self.k2[64:128, :], self.k2[0:64, :], "kdup")

    def l1_bvb(self):
        E_, d = self.E, self.dram
        E_.wait("gpsimd", "pa", E_.c("pa"))
        E_.wait("gpsimd", "sx", E_.c("sx"))
        E_.wait("gpsimd", "vm", E_.c("vm"))
        bvcol = self.pbuf[:, 8:9]
        for lo in (0, 64):
            E_.dma("gpsimd", bvcol[lo:lo + 64, :],
                   d["l1_bv"].rearrange("(p one) -> p one", one=1), "wl")
        E_.wait("vector", "wl", E_.c("wl"))
        vcopy(E_, self.bvb, bcast_free(bvcol, 8), sem="bvv")
        E_.wait("tensor", "bvv", E_.c("bvv"))

    def rwkv(self):
        E_, d = self.E, self.dram
        arb = self.arb
        arb.reset("zoneE")
        wrt = arb.alloc(8192, "wrt")
        wkt = arb.alloc(8192, "wkt")
        wvt = arb.alloc(8192, "wvt")
        xr = arb.alloc(2048, "xr")
        xk = arb.alloc(2048, "xk")
        rbuf = arb.alloc(2048, "rbuf")
        kact = arb.alloc(2048, "kact")
        self.phase_barrier(("sync",))
        E_.wait("sync", "kdup", E_.c("kdup"))
        for mat, scr, dstb in (("wr", "wrs", wrt), ("wk", "wks", wkt), ("wv", "wvs", wvt)):
            E_.wait("sync", "cast", self.cast_mark[mat])
            for cc in range(NCH):
                E_.dma("sync", dstb[:, cc * 1024:(cc + 1) * 1024],
                       d[scr][0:C, cc * 128:(cc + 1) * 128], "wt", transpose=True)
        wt_mark = E_.c("wt")
        for nm, key in (("mur", "r_mur"), ("muk", "r_muk")):
            E_.dma("gpsimd", getattr(self, nm), d[key].rearrange("(cc p) -> p cc", p=128),
                   "kl", allow_slow_non_contiguous=True)
        E_.wait("vector", "kl", E_.c("kl"))
        vts(E_, self.murm, self.mur, -1.0, s2=1.0, op0=ALU.mult, op1=ALU.add, sem="rv")
        vts(E_, self.mukm, self.muk, -1.0, s2=1.0, op0=ALU.mult, op1=ALU.add, sem="rv")
        E_.wait("vector", "rv", E_.c("rv"))
        E_.wait("tensor", "wt", wt_mark)
        E_.wait("tensor", "bnd", E_.c("bnd"))
        TQ = 256
        for tq in range(TOK // TQ):
            E_.wait("vector", "rp", E_.c("rp"))
            for cc in range(NCH):
                xsl = self.xn_ap(cc, tq * TQ, TQ)
                xprev = self.xn[:, cc * (TOK + 1) + tq * TQ: cc * (TOK + 1) + tq * TQ + TQ]
                vts(E_, self.tok[:, 0:TQ], xprev, self.murm[:, cc:cc + 1], op0=ALU.mult)
                vstt(E_, xr[:, cc * TQ:(cc + 1) * TQ], xsl, self.mur[:, cc:cc + 1],
                     self.tok[:, 0:TQ], ALU.mult, ALU.add)
                vts(E_, self.tok[:, 0:TQ], xprev, self.mukm[:, cc:cc + 1], op0=ALU.mult)
                vstt(E_, xk[:, cc * TQ:(cc + 1) * TQ], xsl, self.muk[:, cc:cc + 1],
                     self.tok[:, 0:TQ], ALU.mult, ALU.add, sem="rv")
            E_.wait("tensor", "rv", E_.c("rv"))
            for which, wt_, dst, act_f in (("r", wrt, rbuf, AF.Sigmoid), ("k", wkt, kact, AF.Relu)):
                srcb = xr if which == "r" else xk
                for dc in range(NCH):
                    ps = self.psf[dc % 2]
                    E_.wait("tensor", "rs", self.rs_mark.get(dc % 2, 0))
                    for cc in range(NCH):
                        mm(E_, ps[:, 0:TQ], wt_[:, cc * 1024 + dc * 128: cc * 1024 + (dc + 1) * 128],
                           srcb[:, cc * TQ:(cc + 1) * TQ], start=(cc == 0), stop=(cc == NCH - 1),
                           sem=("rp" if cc == NCH - 1 else None))
                    E_.wait("scalar", "rp", E_.c("rp"))
                    act(E_, dst[:, dc * TQ:(dc + 1) * TQ], ps[:, 0:TQ], act_f, sem="rs")
                    self.rs_mark[dc % 2] = E_.c("rs")
            E_.wait("scalar", "rs", E_.c("rs"))
            act(E_, kact[:, :], kact[:, :], AF.Square, sem="rs")
            E_.wait("tensor", "rs", E_.c("rs"))
            for dc in range(NCH):
                ps = self.psf[2 + dc % 2]
                E_.wait("tensor", "rxv", self.rx_mark.get(dc % 2, 0))
                for cc in range(NCH):
                    mm(E_, ps[:, 0:TQ], wvt[:, cc * 1024 + dc * 128: cc * 1024 + (dc + 1) * 128],
                       kact[:, cc * TQ:(cc + 1) * TQ], start=(cc == 0), stop=(cc == NCH - 1),
                       sem=("rp" if cc == NCH - 1 else None))
                E_.wait("vector", "rp", E_.c("rp"))
                vtt(E_, self.tok[:, TQ:2 * TQ], ps[:, 0:TQ], rbuf[:, dc * TQ:(dc + 1) * TQ],
                    ALU.mult)
                xsl = self.xf[:, dc * TOK + tq * TQ: dc * TOK + tq * TQ + TQ]
                vtt(E_, xsl, xsl, self.tok[:, TQ:2 * TQ], ALU.add, sem="rxv")
                self.rx_mark[dc % 2] = E_.c("rxv")

    def attn_l1(self):
        E_, d = self.E, self.dram
        arb = self.arb
        arb.reset("zoneE")
        self.obuf = arb.alloc(NPAIR * TOK, "obuf")
        self.qhalf = arb.alloc(4 * TOK, "qhalf")
        self.pbuf = arb.alloc(2 * TB, "pbuf")
        self.diagb = arb.alloc(4 * TB, "diagb")
        self.wqkv = arb.alloc(2 * C, "wqkv")
        self.sqscr = arb.alloc(TB, "sqscr")
        self.ostage = arb.alloc(TB, "ostage")
        self.prod_bf = arb.alloc(16, "prod_bf")
        self.rrow_bf = arb.alloc(TB, "rrow")
        self.qtmp = arb.alloc(128, "qtmp")
        self.bvb = self.pbuf[:, 0:8]
        self.onesb = self.diagb[:, 3 * TB + 511: 3 * TB + 512]
        self.phase_barrier(("tensor", "vector", "scalar", "gpsimd", "sync"))
        E_.dma("gpsimd", self.diagb, d["diag"], "cl")
        for nm, key in (("bq", "l1_bq"), ("bo", "l1_bo")):
            E_.dma("gpsimd", self.bias_sl[nm], d[key].rearrange("(cc p) -> p cc", p=128), "cl",
                   allow_slow_non_contiguous=True)
        E_.wait("tensor", "cl", E_.c("cl"))
        E_.wait("vector", "cl", E_.c("cl"))
        E_.wait("scalar", "cl", E_.c("cl"))
        self.vstride = 65
        self.v_head_off = 0
        self.kbuf = self.k2
        self.vbuf = self.vb1
        wq = self.wqkv[:, 0:C]
        import os as _os
        upto = _os.environ.get("L1_UPTO", "all")
        for ph in range(2):
            for pl in range(4):
                p = ph * 4 + pl
                E_.wait("tensor", "obc", E_.c("obc"))
                E_.wait("tensor", "ov", E_.c("ov"))
                E_.wait("tensor", "nv", E_.c("nv"))
                E_.wait("gpsimd", "qpe", E_.c("qpe"))
                src = d["l1_Wq"][0:C, p * 128:(p + 1) * 128].rearrange("(ch cl) o -> cl ch o", cl=128)
                E_.dma("gpsimd", wq.rearrange("q (ch o) -> q ch o", ch=NCH), src, "wl")
                E_.wait("tensor", "wl", E_.c("wl"))
                for it in range(NTILE):
                    ps = self.psf[4 + it % 2]
                    if it >= 2:
                        E_.wait("tensor", "qrv", self.qr_mark[it - 2])
                    for cc in range(NCH):
                        mm(E_, ps[:, 0:128], self.xn_ap(cc, it * 128, 128),
                           wq[:, cc * 128:(cc + 1) * 128], start=(cc == 0), stop=False)
                    mm(E_, ps[:, 0:128], self.onesrow_bf[0:1, :],
                       self.bqrow[0:1, p * 128:(p + 1) * 128], start=False, stop=True, sem="qpe")
                    E_.wait("vector", "qpe", E_.c("qpe"))
                    rsc = self.tok[:, 0:256]
                    tc = ap3(self.tcos, it * 64, [[0, 2], [1, 64]])
                    ps3 = ps[:, 0:128].rearrange("p (h d) -> p h d", h=2)
                    vtt(E_, rsc[:, 0:128].rearrange("p (h d) -> p h d", h=2), ps3, tc, ALU.mult)
                    vtt(E_, ap3(self.tok, 128, [[32, 2], [1, 16]]),
                        ap3(ps[:, 0:128], 16, [[64, 2], [1, 16]]),
                        ap3(self.tsin, it * 64, [[0, 2], [1, 16]]), ALU.mult)
                    vtt(E_, ap3(self.tok, 144, [[32, 2], [1, 16]]),
                        ap3(ps[:, 0:128], 0, [[64, 2], [1, 16]]),
                        ap3(self.tsin, it * 64 + 16, [[0, 2], [1, 16]]), ALU.mult, sem="qrv")
                    E_.wait("vector", "qrv", E_.c("qrv"))
                    vtt(E_, ap3(self.qtmp, 0, [[64, 2], [1, 32]]),
                        ap3(self.tok, 0, [[64, 2], [1, 32]]),
                        ap3(self.tok, 128, [[32, 2], [1, 32]]), ALU.add)
                    vts(E_, ap3(self.qtmp, 32, [[64, 2], [1, 32]]),
                        ap3(self.tok, 32, [[64, 2], [1, 32]]), 1.0, sem="qrv")
                    E_.wait("tensor", "qrv", E_.c("qrv"))
                    bank = self.pst[it % 2]
                    if it >= 2:
                        E_.wait("tensor", "qtv", E_.c("qtv") - 1)
                    transpose(E_, bank[0:128, 0:128], self.qtmp[:, 0:128], self.idb[:, :], sem="qpe")
                    E_.wait("vector", "qpe", E_.c("qpe"))
                    vcopy(E_, self.qhalf[:, pl * TOK + it * 128: pl * TOK + (it + 1) * 128],
                          bank[0:128, 0:128], sem="qtv")
                    self.qr_mark[it] = E_.c("qtv")
            if upto == "qbuild":
                continue
            for pl in range(4):
                p = ph * 4 + pl
                self.qbuf = self.qhalf[:, pl * TOK:(pl + 1) * TOK]
                E_.wait("tensor", "qtv", E_.c("qtv"))
                self.emit_norms(p)
                if upto == "norms":
                    continue
                for hi in range(2):
                    if int(_os.environ.get("L1_HEADS", "99")) > self.oq_flip // 4:
                        self.emit_head(1, p, hi)
        if upto in ("qbuild", "norms", "heads"):
            return
        self.l1_bvb()
        self.emit_wo(1)

    def l1_all(self):
        self.oq_flip = 0
        self.sx_mark = {}
        self.pa_mark = {}
        self.qv_mark = {}
        self.ov_mark = {}
        self.kv_mark = {}
        self.qr_mark = {}
        self.rs_mark = {}
        self.rx_mark = {}
        if not hasattr(self, "bias_sl"):
            self.bias_sl = {}
            s = self.smslot
            for nm in ("bq", "bk", "bo"):
                self.bias_sl[nm] = s(8, nm)
            self.negM = s(32, "negM")
            self.negBIG = s(1, "negBIG")
            self.boeff = s(1, "boeff")
            self.n4 = s(4, "n4")
            self.nq = s(2, "nq")
            self.nq2 = s(1, "nq2")
            memset(self.E, "vector", self.negBIG, -BIGM, sem="cst")
        import os as _os
        upto = _os.environ.get("L1_UPTO", "all")
        if self.mode == "fused":
            self.l1_stage_ag()
            self.l1_ln()
        self.l1_carve()
        self.l1_stream()
        if upto == "stream":
            return self._l1_fin()
        self.l1_kv()
        if upto == "kv":
            return self._l1_fin()
        self.rwkv()
        if upto == "rwkv":
            return self._l1_fin()
        self.attn_l1()
        self._l1_fin()

    def _l1_fin(self):
        self.stage_out_final()
        self.finish(extra_waits=("stg",))


for _nm in ("emit_stage_out", "stage_out_final", "l1_stage_ag", "l1_ln", "l1_carve",
            "l1_stream", "l1_kv", "l1_bvb", "rwkv", "attn_l1", "l1_all", "_l1_fin"):
    setattr(MKProgram, _nm, getattr(L1Mixin, _nm))


from concourse.bass_utils import run_bass_kernel_spmd

_PROGS = {}


def _get_prog(mode):
    if mode not in _PROGS:
        p = MKProgram(mode)
        if mode == "l0":
            phases = ["p0", "emit_casts", "attn_l0", "moe", "stage_out_final",
                      lambda: p.finish(extra_waits=("stg",))]
        elif mode == "l1":
            phases = ["p0", "emit_casts", "l1_all"]
        else:
            phases = ["p0", "emit_casts", "attn_l0", "moe", "l1_all"]
        nc = p.build(phases)
        _PROGS[mode] = (p, nc)
    return _PROGS[mode]


def _weight_map(inputs):
    m = dict(
        l0_Wq=inputs["L0_attn_Wq"], l0_Wk=inputs["L0_attn_Wk"],
        l0_Wv=inputs["L0_attn_Wv"], l0_Wo=inputs["L0_attn_Wo"],
        l0_bq=inputs["L0_attn_bq"], l0_bk=inputs["L0_attn_bk"],
        l0_bv=inputs["L0_attn_bv"], l0_bo=inputs["L0_attn_bo"],
        m_Wg=inputs["L0_ffn_Wg"],
        m_W1=inputs["L0_ffn_W1"], m_b1=inputs["L0_ffn_b1"],
        m_A1=inputs["L0_ffn_A1"], m_B1=inputs["L0_ffn_B1"],
        m_W2=inputs["L0_ffn_W2"], m_b2=inputs["L0_ffn_b2"],
        m_A2=inputs["L0_ffn_A2"], m_B2=inputs["L0_ffn_B2"],
        m_W3=inputs["L0_ffn_W3"], m_b3=inputs["L0_ffn_b3"],
        m_A3=inputs["L0_ffn_A3"], m_B3=inputs["L0_ffn_B3"],
        l1_Wq=inputs["L1_attn_Wq"], l1_Wk=inputs["L1_attn_Wk"],
        l1_Wv=inputs["L1_attn_Wv"], l1_Wo=inputs["L1_attn_Wo"],
        l1_bq=inputs["L1_attn_bq"], l1_bk=inputs["L1_attn_bk"],
        l1_bv=inputs["L1_attn_bv"], l1_bo=inputs["L1_attn_bo"],
        r_Wr=inputs["L1_ffn_W_r"], r_Wk=inputs["L1_ffn_W_k"],
        r_Wv=inputs["L1_ffn_W_v"], r_mur=inputs["L1_ffn_mu_r"],
        r_muk=inputs["L1_ffn_mu_k"],
    )
    return {k: np.ascontiguousarray(v) for k, v in m.items()}


def _in_maps(p, wm, x_owns, x_oths, consts_list):
    maps = []
    declared = set()
    for name, ap in p.dram.items():
        t = ap.tensor
        kind = getattr(t, "kind", None)
        declared.add(name)
    for c in range(8):
        consts = consts_list[c]
        m = dict(x_own=x_owns[c], x_oth=x_oths[c],
                 ident_bf=consts["ident_bf"], ident_f32=consts["ident_f32"],
                 flags=consts["flags"], diag=consts["diag_mask"],
                 rope_cos=consts["rope_cos"], rope_sin=consts["rope_sin"])
        m.update(wm)
        maps.append({k: v for k, v in m.items() if k in p.dram and _is_ext_in(p, k)})
    return maps


def _is_ext_in(p, name):
    return name in p.ext_inputs


def kernel(**inputs):
    x = np.ascontiguousarray(inputs["L0_attn_x"], dtype=np.float32)
    wm = _weight_map(inputs)
    consts_list = [host_constants(c) for c in range(8)]
    x_owns, x_oths = [], []
    for c in range(8):
        b, half = c // 2, c % 2
        x_owns.append(np.ascontiguousarray(x[b, half * TOK:(half + 1) * TOK]))
        x_oths.append(np.ascontiguousarray(x[b, (1 - half) * TOK:(2 - half) * TOK]))
    p0_, nc0 = _get_prog("l0")
    res0 = run_bass_kernel_spmd(nc0, _in_maps(p0_, wm, x_owns, x_oths, consts_list),
                                core_ids=list(range(8)))
    x1_owns = [np.ascontiguousarray(r["out"]) for r in res0.results]
    x1_oths = [x1_owns[c ^ 1] for c in range(8)]
    p1_, nc1 = _get_prog("l1")
    res1 = run_bass_kernel_spmd(nc1, _in_maps(p1_, wm, x1_owns, x1_oths, consts_list),
                                core_ids=list(range(8)))
    out = np.empty((4, FULL, C), np.float32)
    for c in range(8):
        b, half = c // 2, c % 2
        out[b, half * TOK:(half + 1) * TOK] = res1.results[c]["out"]
    return out
